# version 01 (final)
# baseline (speedup 1.0000x reference)
import contextlib
import numpy as np
import ml_dtypes
import concourse.bass as bass
import concourse.mybir as mybir
from concourse.bass_utils import run_bass_kernel_spmd

F32 = mybir.dt.float32
BF16 = mybir.dt.bfloat16
I32 = mybir.dt.int32
AF = mybir.ActivationFunctionType
ALU = mybir.AluOpType
AXX = mybir.AxisListType.X

D = 4096
TOK = 1024
NT = 8
NI = 32
NEG = -120.0
EPS = 1e-5
ISQ = float(1.0 / np.sqrt(128.0))

ENGS = ("sync", "gpsimd", "tensor", "vector", "scalar")


class Ctr:
    __slots__ = ("h", "n")

    def __init__(self, h):
        self.h = h
        self.n = 0


def build_nc(upto=6):
    import os
    NT0 = int(os.environ.get("MK_NT0", str(NT)))
    F_STATS = int(os.environ.get("MK_STATS", "1"))
    F_XN = int(os.environ.get("MK_XN", "1"))
    F_TP = int(os.environ.get("MK_TP", "1"))
    F_PD = int(os.environ.get("MK_PD", "1"))
    F_EV = int(os.environ.get("MK_EV", "1"))
    F_GATES = int(os.environ.get("MK_GATES", "1"))
    NJ = int(os.environ.get("MK_NJ", str(NI)))
    F_XN1 = int(os.environ.get("MK_XN1", "1"))
    F_DBG = int(os.environ.get("MK_DBG", "1"))
    G2 = int(os.environ.get("MK_G2", "5"))
    P3L = int(os.environ.get("MK_P3", "5"))
    F_CC = int(os.environ.get("MK_CC", "1"))
    P3K = int(os.environ.get("MK_P3K", "3"))
    F_SILU = int(os.environ.get("MK_SILU", "1"))
    F_STM = int(os.environ.get("MK_STM", "1"))
    F_SPILL = int(os.environ.get("MK_SPILL", "1"))
    F_LNID = int(os.environ.get("MK_LNID", "0"))
    nc = bass.Bass("TRN2", target_bir_lowering=False)

    def din(name, shape, dt=F32):
        return nc.dram_tensor(name, shape, dt, kind="ExternalInput").ap()

    x0_in = din("x0_in", [NT, 128, D])
    kc_in = din("kc_in", [512, 16384])
    vc_in = din("vc_in", [512, 16384])
    btc_in = din("btc_in", [128, 2], I32)
    wq0_in = din("wq0_in", [32, 128, D])
    wo0_in = din("wo0_in", [32, 128, D])
    wg0_in = din("wg0_in", [32, 128, 33 * 128])
    bfm_in = din("bfm_in", [128, 32])
    wrt_in = din("wrt_in", [128, 128])
    att_in = din("att_in", [128, 1024])
    e8_in = din("e8_in", [4, 32])
    wq1_in = din("wq1_in", [32, 128, D])
    wk1_in = din("wk1_in", [8, 128, D])
    wv1_in = din("wv1_in", [8, 128, D])
    wo1_in = din("wo1_in", [32, 128, D])
    wg1_in = din("wg1_in", [32, 128, 36 * 128])
    ld1_in = din("ld1_in", [4, 128, D])
    wqm_in = din("wqm_in", [128, D])
    keys_in = din("keys_in", [128, 8])
    sel_in = din("sel_in", [8, 512])
    t0_in = din("t0_in", [128, 2048])
    t1q_in = din("t1q_in", [128, 2048])
    t1k_in = din("t1k_in", [128, 2048])
    mk_in = din("mk_in", [16, 128, 128])
    idf_in = din("idf_in", [128, 128])

    y_out = nc.dram_tensor("y_out", [NT, 128, D], F32, kind="ExternalOutput").ap()

    xn_scr = nc.dram_tensor("xn_scr", [TOK, D], BF16, kind="Internal").ap()
    x0f_scr = nc.dram_tensor("x0f_scr", [NI, 128, TOK], F32, kind="Internal").ap()
    x1f_scr_t = nc.dram_tensor("x1f_scr", [NI + 1, 128, TOK], F32, kind="Internal")
    x1f_scr = x1f_scr_t.ap()[1:NI + 1]
    cin_kv_t = nc.dram_tensor("cin_kv", [2048, 1024], BF16, kind="Internal")
    cout_kv_t = nc.dram_tensor("cout_kv", [4096, 1024], BF16, kind="Internal")
    cin_m_t = nc.dram_tensor("cin_m", [128, 32], F32, kind="Internal")
    cout_m_t = nc.dram_tensor("cout_m", [128, 32], F32, kind="Internal")
    cin_kv, cout_kv = cin_kv_t.ap(), cout_kv_t.ap()
    cout_m = cout_m_t.ap()
    cin_m = cin_m_t.ap()

    groups = [[0, 1], [2, 3], [4, 5], [6, 7]]

    EMIT = {e: [] for e in ENGS}

    def emit(eng, fn):
        EMIT[eng].append(fn)

    def wt(eng, ctr, val=None):
        v = ctr.n if val is None else val
        if v and v > 0:
            emit(eng, lambda e, s=ctr.h, vv=v: e.wait_ge(s, vv))

    def op(eng, build, inc=None, inc_n=1):
        mark = None
        if inc is not None:
            inc.n += inc_n
            mark = inc.n

        def f(e, b=build, i=inc, n=inc_n):
            ins = b(e)
            if i is not None:
                ins.then_inc(i.h, n)
        emit(eng, f)
        return mark

    with contextlib.ExitStack() as ctx:
        E = ctx.enter_context
        guard = E(nc.sbuf_tensor("guard", [128, 570], F32))
        xnf = E(nc.sbuf_tensor("xnf", [128, NI * TOK], BF16))
        ctxf = E(nc.sbuf_tensor("ctxf", [128, NI * TOK], BF16))
        wS = [E(nc.sbuf_tensor("wS0", [128, 4608], BF16)),
              E(nc.sbuf_tensor("wS1", [128, 4608], BF16)),
              E(nc.sbuf_tensor("wS2", [128, 4096], BF16))]
        kvK = E(nc.sbuf_tensor("kvK", [128, 4096], BF16))
        kvV = E(nc.sbuf_tensor("kvV", [128, 4096], BF16))
        ovl = E(nc.sbuf_tensor("ovl", [128, 4096], F32))
        sc1 = E(nc.sbuf_tensor("sc1", [128, 1024], F32))
        sc2 = E(nc.sbuf_tensor("sc2", [128, 1024], F32))
        sc3 = E(nc.sbuf_tensor("sc3", [128, 1024], F32))
        ptA = E(nc.sbuf_tensor("ptA", [128, 1024], BF16))
        ptB = E(nc.sbuf_tensor("ptB", [128, 1024], BF16))
        bfm = E(nc.sbuf_tensor("bfm", [128, 32], F32))
        e8s = E(nc.sbuf_tensor("e8s", [4, 32], F32))
        sels = E(nc.sbuf_tensor("sels", [8, 512], BF16))
        keyss = E(nc.sbuf_tensor("keyss", [128, 8], BF16))
        btcs = E(nc.sbuf_tensor("btcs", [128, 2], I32))
        idf = E(nc.sbuf_tensor("idf", [128, 128], F32))
        ones_bf = E(nc.sbuf_tensor("ones_bf", [128, 128], BF16))
        onesc = E(nc.sbuf_tensor("onesc", [128, 1], F32))
        epsc = E(nc.sbuf_tensor("epsc", [128, 1], F32))
        ones1f = E(nc.sbuf_tensor("ones1f", [1, 128], F32))
        ut = E(nc.sbuf_tensor("ut", [128, 64], F32))
        canhi = E(nc.sbuf_tensor("canhi", [128, 184], F32))
        pA = E(nc.psum_tensor("pA", [128, 1024], F32))
        pB = E(nc.psum_tensor("pB", [128, 1024], F32))
        pC = E(nc.psum_tensor("pC", [128, 1024], F32))
        pD = E(nc.psum_tensor("pD", [128, 1024], F32))
        dG = Ctr(E(nc.semaphore("dG")))
        dS = Ctr(E(nc.semaphore("dS")))
        sT = Ctr(E(nc.semaphore("sT")))
        sV = Ctr(E(nc.semaphore("sV")))
        sA = Ctr(E(nc.semaphore("sA")))
        sP = Ctr(E(nc.semaphore("sP")))
        cc = Ctr(E(nc.semaphore("cc")))
        block = E(nc.Block())

        ovl2 = ovl[:, 2048:3072]
        ovl5 = ovl[:, 0:1024]
        hps = [kvK[:, 2048:3072], kvK[:, 3072:4096],
               kvV[:, 2048:3072], kvV[:, 3072:4096]]

        def dmaS(dst, src, transpose=False):
            if transpose:
                return op("sync", lambda e, d=dst, s=src: e.dma_start(out=d, in_=s, transpose=True),
                          inc=dS, inc_n=16)
            return op("sync", lambda e, d=dst, s=src: e.dma_start(out=d, in_=s), inc=dS, inc_n=16)

        def dmaG(dst, src):
            return op("gpsimd", lambda e, d=dst, s=src: e.dma_start(out=d, in_=s), inc=dG, inc_n=16)

        def mm(psum_ap, lhsT_ap, rhs_ap, start, stop, inc=None):
            return op("tensor", lambda e, o=psum_ap, l=lhsT_ap, r=rhs_ap, st=start, sp=stop:
                      e.matmul(o, l, r, start=st, stop=sp), inc=inc)

        dmaS(btcs[:], btc_in)
        dmaS(idf[:], idf_in)
        dmaS(bfm[:], bfm_in)
        c_smallS = dmaS(e8s[:], e8_in)
        dmaG(sels[:], sel_in)
        c_smallG = dmaG(keyss[:], keys_in)
        op("gpsimd", lambda e: e.memset(ones_bf[:], 1.0))
        op("gpsimd", lambda e: e.memset(onesc[:], 1.0))
        op("gpsimd", lambda e: e.memset(epsc[:], EPS))
        op("gpsimd", lambda e: e.memset(canhi[:], 123.0))
        op("gpsimd", lambda e: e.memset(guard[:], 321.0))
        consts_ready = op("gpsimd", lambda e: e.memset(ones1f[:], 1.0), inc=sP)

        tp_slot_free = [0, 0, 0, 0]
        pd_mark = [0, 0]
        last_tp_mark = 0
        last_xn_mark = 0
        xn_wr_last = 0
        for t in range(NT0):
            if t > 0:
                wt("sync", sT, last_tp_mark)
                wt("sync", sV, last_xn_mark)
            c_x0 = dmaS(ovl[:], x0_in[t])
            wt("vector", dS, c_x0)
            if not F_STATS:
                op("vector", lambda e: e.memset(ut[:, 0:16], 0.5), inc=sV)
                m_var = sV.n
                op("scalar", lambda e: e.copy(ut[:, 7:8], ut[:, 6:7]))
                m_rstd = op("scalar", lambda e: e.copy(ut[:, 7:8], ut[:, 6:7]), inc=sA)
            if F_STATS:
                m_rd0 = op("vector", lambda e: e.reduce_sum(ut[:, 0:1], ovl[:], axis=AXX), inc=sV)
            for q in range(4 if F_STATS else 0):
                wt("scalar", dS, c_x0)
                if q > 0 or t > 0:
                    wt("scalar", sV, m_rdq)
                m_sqq = op("scalar", lambda e, q=q: e.activation(
                    out=sc1[:], in_=ovl[:, q * 1024:(q + 1) * 1024], func=AF.Square), inc=sA)
                wt("vector", sA, m_sqq)
                m_rdq = op("vector", lambda e, q=q: e.reduce_sum(ut[:, 2 + q:3 + q], sc1[:], axis=AXX), inc=sV)
            if F_STATS:
              wt("vector", sV, m_rdq)
              m0 = op("vector", lambda e: e.tensor_scalar(out=ut[:, 1:2], in0=ut[:, 0:1],
               scalar1=1.0 / D, scalar2=None, op0=ALU.mult), inc=sV)
              m1 = op("vector", lambda e: e.tensor_tensor(out=ut[:, 11:12], in0=ut[:, 2:3],
               in1=ut[:, 3:4], op=ALU.add), inc=sV)
              m2 = op("vector", lambda e: e.tensor_tensor(out=ut[:, 12:13], in0=ut[:, 4:5],
               in1=ut[:, 5:6], op=ALU.add), inc=sV)
              wt("vector", sV, m2)
              m3 = op("vector", lambda e: e.tensor_tensor(out=ut[:, 13:14], in0=ut[:, 11:12],
               in1=ut[:, 12:13], op=ALU.add), inc=sV)
              wt("vector", sV, m3)
              m4 = op("vector", lambda e: e.tensor_scalar(out=ut[:, 14:15], in0=ut[:, 13:14],
               scalar1=1.0 / D, scalar2=None, op0=ALU.mult), inc=sV)
              m5 = op("vector", lambda e: e.tensor_scalar(out=ut[:, 15:16], in0=ut[:, 1:2],
               scalar1=ut[:, 1:2], scalar2=None, op0=ALU.mult), inc=sV)
              wt("vector", sV, m5)
              m_var = op("vector", lambda e: e.tensor_tensor(out=ut[:, 6:7], in0=ut[:, 14:15],
               in1=ut[:, 15:16], op=ALU.subtract), inc=sV)
              wt("scalar", sV, m_var)
              m_ln = op("scalar", lambda e: e.activation(out=ut[:, 10:11], in_=ut[:, 6:7],
               func=AF.Ln, bias=epsc[:]), inc=sA)
              wt("scalar", sA, m_ln)
              m_rstd = op("scalar", lambda e: e.activation(out=ut[:, 7:8], in_=ut[:, 10:11],
               func=AF.Exp, scale=-0.5), inc=sA)
            wt("vector", sA, m_rstd)
            for hh in range(2 if F_XN else 0):
                if xn_wr_last:
                    wt("vector", dS, xn_wr_last)
                if F_LNID:
                    op("vector", lambda e, hh=hh: e.tensor_copy(ptA[:],
                       ovl[:, hh * 2048:hh * 2048 + 1024]))
                    m_xn = op("vector", lambda e, hh=hh: e.tensor_copy(ptB[:],
                       ovl[:, hh * 2048 + 1024:hh * 2048 + 2048]), inc=sV)
                else:
                    op("vector", lambda e, hh=hh: e.tensor_scalar(out=ptA[:],
                       in0=ovl[:, hh * 2048:hh * 2048 + 1024],
                       scalar1=ut[:, 1:2], scalar2=ut[:, 7:8], op0=ALU.subtract, op1=ALU.mult))
                    m_xn = op("vector", lambda e, hh=hh: e.tensor_scalar(out=ptB[:],
                       in0=ovl[:, hh * 2048 + 1024:hh * 2048 + 2048],
                       scalar1=ut[:, 1:2], scalar2=ut[:, 7:8], op0=ALU.subtract, op1=ALU.mult), inc=sV)
                wt("sync", sV, m_xn)
                dmaS(xn_scr[t * 128:(t + 1) * 128, hh * 2048:hh * 2048 + 1024], ptA[:])
                xn_wr_last = dmaS(xn_scr[t * 128:(t + 1) * 128, hh * 2048 + 1024:hh * 2048 + 2048], ptB[:])
                last_xn_mark = m_xn
            wt("tensor", dS, c_x0)
            wt("tensor", sP, consts_ready)
            wt("tensor", sV, last_xn_mark)
            for g in range(8 if F_TP else 0):
                slot = g % 2
                pp = pA if slot == 0 else pB
                if tp_slot_free[slot]:
                    wt("tensor", sV, tp_slot_free[slot])
                m_tp = None
                for u in range(4):
                    j = g * 4 + u
                    m_tp = op("tensor", lambda e, j=j, pp=pp, u=u: e.transpose(
                        pp[:, u * 128:(u + 1) * 128], ovl[:, j * 128:(j + 1) * 128], idf[:]),
                        inc=(sT if u == 3 else None))
                if F_EV:
                    wt("vector", sT, m_tp)
                    if pd_mark[slot]:
                        wt("vector", dS, pd_mark[slot])
                    m_ev = op("vector", lambda e, g=g, pp=pp: e.tensor_copy(
                        sc2[:, (g % 2) * 512:(g % 2) * 512 + 512], pp[:, 0:512]), inc=sV)
                    tp_slot_free[slot] = m_ev
                    if F_PD:
                        wt("sync", sV, m_ev)
                        pd_mark[slot] = dmaS(
                            x0f_scr[g * 4:(g + 1) * 4, :, t * 128:(t + 1) * 128]
                            .rearrange("a p c -> p a c"),
                            sc2[:, (g % 2) * 512:(g % 2) * 512 + 512]
                            .rearrange("p (a c) -> p a c", a=4))
                last_tp_mark = m_tp

        if upto <= 0:
            wt("sync", dS, dS.n)
            _flush(block, EMIT)
            return nc
        for j in range(NI):
            dmaS(xnf[:, j * TOK:(j + 1) * TOK], xn_scr[:, j * 128:(j + 1) * 128], transpose=True)
        c_xnf = dS.n
        wt("sync", sT, last_tp_mark)
        wt("sync", sV, last_xn_mark)
        c_tab0 = dmaS(ovl[:, 0:2048], t0_in)

        kvV_free = 0
        kvK_free = 0
        gstage_free_T = last_tp_mark
        gstage_free_V = last_xn_mark
        sums_evac = 0
        ws_free = [0, 0, 0]
        wt("gpsimd", dS, c_xnf)
        wt("gpsimd", sV, gstage_free_V)
        wt("gpsimd", sT, gstage_free_T)
        op("gpsimd", lambda e: e.indirect_dma_start(
            out=ovl[:, 2048:4096], out_offset=None, in_=vc_in,
            in_offset=bass.IndirectOffsetOnAxis(btcs[:, 0:1], 0),
            element_offset=0), inc=dG, inc_n=16)
        for h in range(8):
            if sums_evac:
                wt("tensor", sV, sums_evac)
            for half in range(2):
                wt("gpsimd", sV, gstage_free_V)
                wt("gpsimd", sT, gstage_free_T)
                wt("gpsimd", sT, kvV_free)
                op("gpsimd", lambda e, h=h, half=half: e.indirect_dma_start(
                    out=ovl[:, 2048:4096], out_offset=None, in_=vc_in,
                    in_offset=bass.IndirectOffsetOnAxis(btcs[:, half:half + 1], 0),
                    element_offset=h * 2048), inc=dG, inc_n=16)
                wt("vector", dG)
                gstage_free_V = op("vector", lambda e, half=half: e.tensor_copy(
                    kvV[:, half * 2048:(half + 1) * 2048], ovl[:, 2048:4096]), inc=sV)
                wt("gpsimd", sV, gstage_free_V)
                wt("gpsimd", sT, kvK_free)
                op("gpsimd", lambda e, h=h, half=half: e.indirect_dma_start(
                    out=ovl[:, 2048:4096], out_offset=None, in_=kc_in,
                    in_offset=bass.IndirectOffsetOnAxis(btcs[:, half:half + 1], 0),
                    element_offset=h * 2048), inc=dG, inc_n=16)
                wt("tensor", dG)
                m_tp = 0
                for g in range(4):
                    slot = g % 2
                    pp = pC if slot == 0 else pD
                    if tp_slot_free[slot]:
                        wt("tensor", sV, tp_slot_free[slot])
                    for u in range(4):
                        pz = g * 4 + u
                        m_tp = op("tensor", lambda e, pz=pz, pp=pp, u=u: e.transpose(
                            pp[:, u * 128:(u + 1) * 128],
                            ovl[:, 2048 + pz * 128:2048 + (pz + 1) * 128], idf[:]),
                            inc=(sT if u == 3 else None))
                    wt("vector", sT, m_tp)
                    kvcol = half * 2048 + g * 512
                    tp_slot_free[slot] = op("vector", lambda e, kvcol=kvcol, pp=pp: e.tensor_copy(
                        kvK[:, kvcol:kvcol + 512], pp[:, 0:512]), inc=sV)
                gstage_free_T = m_tp
            kv_ready = sV.n
            tp_slot_free = [0, 0, 0, 0]

            for qi in range(4):
                hq = 4 * h + qi
                wb = wS[hq % 3]
                if ws_free[hq % 3]:
                    wt("gpsimd", sT, ws_free[hq % 3])
                c_w = dmaG(wb[:, 0:4096], wq0_in[hq])
                wt("tensor", dG, c_w)
                wt("tensor", dS, c_xnf)
                if sums_evac:
                    wt("tensor", sV, sums_evac)
                q_mm = None
                for half in range(2):
                    for ic in range(NI):
                        q_mm = mm(pA[:, half * 512:(half + 1) * 512],
                                  wb[:, ic * 128:(ic + 1) * 128],
                                  xnf[:, ic * TOK + half * 512:ic * TOK + (half + 1) * 512],
                                  start=(ic == 0), stop=(ic == NI - 1),
                                  inc=(sT if (ic == NI - 1 and half == 1) else None))
                ws_free[hq % 3] = q_mm
                wt("vector", sT, q_mm)
                wt("vector", dS, c_tab0)
                op("vector", lambda e: e.tensor_tensor(out=sc3[:], in0=pA[:], in1=ovl[:, 0:1024], op=ALU.mult))
                op("vector", lambda e: e.tensor_tensor(out=sc1[0:64, :], in0=pA[64:128, :],
                   in1=ovl[0:64, 1024:2048], op=ALU.mult))
                op("vector", lambda e: e.tensor_tensor(out=sc1[64:128, :], in0=pA[0:64, :],
                   in1=ovl[64:128, 1024:2048], op=ALU.mult))
                q_rope = op("vector", lambda e, hq=hq: e.tensor_tensor(
                    out=ctxf[:, hq * TOK:(hq + 1) * TOK], in0=sc3[:], in1=sc1[:], op=ALU.add), inc=sV)

                wt("tensor", sV, max(q_rope, kv_ready))
                st_marks = {}
                exp_marks = {}
                pv_marks = {}

                def emit_st(c, hq=hq, st_marks=st_marks, exp_marks=exp_marks):
                    ps = pB if c % 2 == 0 else pC
                    if c >= 2:
                        wt("tensor", sA, exp_marks[c - 2])
                    for half in range(2):
                        st_marks[c] = mm(ps[:, half * 512:(half + 1) * 512],
                                         kvK[:, c * 128:(c + 1) * 128],
                                         ctxf[:, hq * TOK + half * 512:hq * TOK + (half + 1) * 512],
                                         start=True, stop=True,
                                         inc=(sT if half == 1 else None))

                def emit_exp(c, st_marks=st_marks, exp_marks=exp_marks, pv_marks=pv_marks):
                    ps = pB if c % 2 == 0 else pC
                    pt = ptA if c % 2 == 0 else ptB
                    wt("scalar", sT, st_marks[c])
                    if c >= 2:
                        wt("scalar", sT, pv_marks[c - 2])
                    exp_marks[c] = op("scalar", lambda e, ps=ps, pt=pt: e.activation(
                        out=pt[:], in_=ps[:], func=AF.Exp), inc=sA)

                def emit_pv(c, exp_marks=exp_marks, pv_marks=pv_marks):
                    pt = ptA if c % 2 == 0 else ptB
                    wt("tensor", sA, exp_marks[c])
                    for half in range(2):
                        mm(pD[:, half * 512:(half + 1) * 512], kvV[:, c * 128:(c + 1) * 128],
                           pt[:, half * 512:(half + 1) * 512], start=(c == 0), stop=(c == 31))
                        pv_marks[c] = mm(pA[:, half * 512:(half + 1) * 512], ones_bf[:],
                                         pt[:, half * 512:(half + 1) * 512],
                                         start=(c == 0), stop=(c == 31),
                                         inc=(sT if half == 1 else None))

                emit_st(0)
                emit_exp(0)
                for c in range(1, 32):
                    emit_st(c)
                    emit_exp(c)
                    emit_pv(c - 1)
                emit_pv(31)
                kvK_free = st_marks[31]
                kvV_free = pv_marks[31]
                wt("vector", sT, pv_marks[31])
                wt("vector", sV, sV.n)
                m_rc = op("vector", lambda e: e.reciprocal(sc3[:], pA[:]), inc=sV)
                wt("vector", sV, m_rc)
                sums_evac = op("vector", lambda e, hq=hq: e.tensor_tensor(
                    out=ctxf[:, hq * TOK:(hq + 1) * TOK], in0=pD[:], in1=sc3[:],
                    op=ALU.mult), inc=sV)

        p1_v = sV.n
        if upto <= 1:
            wt("sync", sV, p1_v)
            _flush(block, EMIT)
            return nc
        ws_free = [sT.n] * 3

        if not F_GATES:
            wt("vector", sV, p1_v)
            op("vector", lambda e: e.memset(ptA[0:32, :], 0.0))
            rp_ready = op("vector", lambda e: e.memset(sc1[:, 0:8], 0.0), inc=sV)
        if F_GATES:
          wt("gpsimd", sT, kvV_free)
          c_wrt = dmaG(ptB[:, 0:128], wrt_in)
          wt("gpsimd", sT, ws_free[2])
          c_att = dmaG(wS[2][:, 0:1024], att_in)
          wt("tensor", sV, p1_v)
          wt("tensor", dG, c_wrt)
          m_g = None
          for half in range(2):
              for ic in range(NI):
                  m_g = mm(pD[0:4, half * 512:(half + 1) * 512], ptB[:, ic * 4:(ic + 1) * 4],
                           xnf[:, ic * TOK + half * 512:ic * TOK + (half + 1) * 512],
                           start=(ic == 0), stop=(ic == NI - 1),
                           inc=(sT if (ic == NI - 1 and half == 1) else None))
          wt("vector", sT, m_g)
          m_gv = op("vector", lambda e: e.tensor_copy(sc3[0:4, :], pD[0:4, :]), inc=sV)
          if G2 == 0:
              rp_ready = op("vector", lambda e: e.memset(ptA[0:32, :], 0.25), inc=sV)
          wt("tensor", sV, m_gv)
          gsl = [0, 0]
          for t in range(NT):
              pp = pA if t % 2 == 0 else pB
              if gsl[t % 2]:
                  wt("tensor", sV, gsl[t % 2])
              m_tp = op("tensor", lambda e, t=t, pp=pp: e.transpose(
                  pp[:, 0:4], sc3[0:4, t * 128:(t + 1) * 128], idf[0:4, 0:4]), inc=sT)
              wt("vector", sT, m_tp)
              gsl[t % 2] = op("vector", lambda e, t=t, pp=pp: e.tensor_copy(
                  sc1[:, t * 4:(t + 1) * 4], pp[:, 0:4]), inc=sV)
          if G2 == 1:
              rp_ready = op("vector", lambda e: e.memset(ptA[0:32, :], 0.25), inc=sV)
          for t in range(NT if G2 >= 2 else 0):
              m_mx = op("vector", lambda e, t=t: e.reduce_max(ut[:, 8:9], sc1[:, t * 4:(t + 1) * 4],
                 axis=AXX, negate=True), inc=sV)
              wt("scalar", sV, m_mx)
              m_ex = op("scalar", lambda e, t=t: e.activation(out=sc2[:, t * 4:(t + 1) * 4],
                 in_=sc1[:, t * 4:(t + 1) * 4], func=AF.Exp, bias=ut[:, 8:9]), inc=sA)
              wt("vector", sA, m_ex)
              m_gsum = op("vector", lambda e, t=t: e.reduce_sum(ut[:, 9:10], sc2[:, t * 4:(t + 1) * 4], axis=AXX), inc=sV)
              wt("vector", sV, m_gsum)
              m_grec = op("vector", lambda e: e.reciprocal(ut[:, 10:11], ut[:, 9:10]), inc=sV)
              wt("vector", sV, m_grec)
              op("vector", lambda e, t=t: e.tensor_scalar(out=sc2[:, 128 + t * 4:128 + (t + 1) * 4],
                 in0=sc2[:, t * 4:(t + 1) * 4], scalar1=ut[:, 10:11], scalar2=None,
                 op0=ALU.mult), inc=sV)
          if G2 == 2:
              rp_ready = op("vector", lambda e: e.memset(ptA[0:32, :], 0.25), inc=sV)
          gate_sm = sV.n
          wt("tensor", sV, gate_sm)
          gsl = [0, 0]
          for t in range(NT if G2 >= 3 else 0):
              pp = pA if t % 2 == 0 else pB
              if gsl[t % 2]:
                  wt("tensor", sV, gsl[t % 2])
              m_tp = mm(pp[0:4, 0:128], sc2[:, 128 + t * 4:128 + (t + 1) * 4], idf[:],
                        start=True, stop=True, inc=sT)
              wt("vector", sT, m_tp)
              gsl[t % 2] = op("vector", lambda e, t=t, pp=pp: e.tensor_copy(
                  sc3[0:4, t * 128:(t + 1) * 128], pp[0:4, 0:128]), inc=sV)
          if G2 == 3:
              rp_ready = op("vector", lambda e: e.memset(ptA[0:32, :], 0.25), inc=sV)
          gtr = sV.n
          wt("tensor", sV, gtr)
          wt("tensor", dS, c_smallS)
          for half in range(2 if G2 >= 4 else 0):
              mm(pD[0:32, half * 512:(half + 1) * 512], e8s[:],
                 sc3[0:4, half * 512:(half + 1) * 512], start=True, stop=True,
                 inc=(sT if half == 1 else None))
          wt("tensor", dG, c_att)
          m_r = None
          for half in range(2 if G2 >= 5 else 0):
              for ic in range(NI):
                  m_r = mm(pC[0:32, half * 512:(half + 1) * 512], wS[2][:, ic * 32:(ic + 1) * 32],
                           xnf[:, ic * TOK + half * 512:ic * TOK + (half + 1) * 512],
                           start=(ic == 0), stop=(ic == NI - 1),
                           inc=(sT if (ic == NI - 1 and half == 1) else None))
          if G2 >= 5:
            ws_free[2] = m_r
            wt("vector", sT, m_r)
            op("vector", lambda e: e.tensor_copy(sc3[0:32, :], pD[0:32, :]))
            op("vector", lambda e: e.tensor_copy(sc2[0:32, :], pC[0:32, :]))
            rp_ready = op("vector", lambda e: e.tensor_tensor(out=ptA[0:32, :], in0=sc2[0:32, :],
               in1=sc3[0:32, :], op=ALU.mult), inc=sV)
          elif G2 == 4:
            rp_ready = op("vector", lambda e: e.memset(ptA[0:32, :], 0.25), inc=sV)

        pA_free, pA_free_ctr = rp_ready, sV
        pB_free, pB_free_ctr = rp_ready, sV
        ovl2_free_T = 0
        ovl2_free_D = 0
        sc1_free_T = 0
        m_st = None
        wt("sync", sV, rp_ready)
        dmaS(x1f_scr_t.ap()[0], ovl[:, 3072:4096])
        dmaS(sc3[:], x0f_scr[0])
        for j in range(NJ):
            wg_buf = wS[j % 2]
            wt("gpsimd", sT, ws_free[j % 2])
            c_wg = dmaG(wg_buf[:, 0:4224], wg0_in[j])
            wo_buf = wS[2] if j > 0 else wS[1]
            wt("gpsimd", sT, ws_free[2 if j > 0 else 1])
            c_wo = dmaG(wo_buf[:, 0:4096], wo0_in[j])
            wt("tensor", dG, c_wg)
            wt("tensor", pA_free_ctr, pA_free)
            m_gm = None
            for half in range(2):
                for ic in range(33):
                    rhs = (xnf[:, ic * TOK + half * 512:ic * TOK + (half + 1) * 512]
                           if ic < 32 else ptA[:, half * 512:(half + 1) * 512])
                    m_gm = mm(pA[:, half * 512:(half + 1) * 512],
                              wg_buf[:, ic * 128:(ic + 1) * 128], rhs,
                              start=(ic == 0), stop=(ic == 32),
                              inc=(sT if (ic == 32 and half == 1) else None))
            ws_free[j % 2] = m_gm
            wt("tensor", dG, c_wo)
            wt("tensor", pB_free_ctr, pB_free)
            m_sm = None
            for half in range(2):
                for ic in range(NI):
                    m_sm = mm(pB[:, half * 512:(half + 1) * 512],
                              wo_buf[:, ic * 128:(ic + 1) * 128],
                              ctxf[:, ic * TOK + half * 512:ic * TOK + (half + 1) * 512],
                              start=(ic == 0), stop=(ic == NI - 1),
                              inc=(sT if (ic == NI - 1 and half == 1) else None))
            ws_free[2 if j > 0 else 1] = m_sm
            wt("scalar", sT, m_gm)
            if sc1_free_T:
                wt("scalar", sT, sc1_free_T)
            if F_SILU == 1:
                m_silu = op("scalar", lambda e, j=j: e.activation(out=sc1[:], in_=pA[:], func=AF.Silu,
                   bias=bfm[:, j:j + 1]), inc=sA)
            elif F_SILU == 2:
                m_silu = op("scalar", lambda e: e.activation(out=sc1[:], in_=pA[:], func=AF.Silu), inc=sA)
            elif F_SILU == 3:
                m_silu = op("scalar", lambda e, j=j: e.activation(out=sc1[:], in_=pA[:], func=AF.Exp,
                   bias=bfm[:, j:j + 1]), inc=sA)
            elif F_SILU == 4:
                m_silu = op("scalar", lambda e: e.mul(sc1[:], pA[:], 0.0), inc=sA)
                wt("scalar", sA, m_silu)
                m_silu = op("scalar", lambda e: e.add(sc1[:], sc1[:], 1.0), inc=sA)
            else:
                m_silu = op("scalar", lambda e: e.copy(sc1[:], pA[:]), inc=sA)
            pA_free, pA_free_ctr = m_silu, sA
            if ovl2_free_T:
                wt("sync", sT, ovl2_free_T)
            if ovl2_free_D:
                wt("sync", dS, ovl2_free_D)
            c_x0p = dmaS(sc3[:], x0f_scr[j])
            wt("vector", sA, m_silu)
            wt("vector", sT, m_sm)
            wt("vector", dS, c_x0p)
            ovl2b = ovl[:, 3072:4096]
            op("vector", lambda e: e.tensor_tensor(out=sc2[:], in0=pB[:], in1=sc1[:], op=ALU.mult))
            m_x1 = op("vector", lambda e, o2=ovl2b: e.tensor_tensor(out=o2, in0=sc2[:], in1=sc3[:],
               op=ALU.add), inc=sV)
            pB_free, pB_free_ctr = m_x1, sV
            wt("sync", sV, m_x1)
            if F_SPILL:
                ovl2_free_D = dmaS(x1f_scr[j], ovl2b)
            wt("tensor", sV, m_x1)
            if F_STM:
                for half in range(2):
                    mm(pC[0:1, half * 512:(half + 1) * 512], onesc[:],
                       ovl2b[:, half * 512:(half + 1) * 512], start=(j == 0), stop=(j == NJ - 1))
                wt("scalar", sV, m_x1)
                m_sq = op("scalar", lambda e, o2=ovl2b: e.activation(out=sc1[:], in_=o2, func=AF.Square), inc=sA)
                wt("tensor", sA, m_sq)
                for half in range(2):
                    m_st = mm(pD[32:33, half * 512:(half + 1) * 512], onesc[:],
                              sc1[:, half * 512:(half + 1) * 512], start=(j == 0), stop=(j == NJ - 1),
                              inc=(sT if half == 1 else None))
            else:
                m_sq = op("vector", lambda e: e.tensor_copy(sc1[:, 0:8], ovl2[:, 0:8]), inc=sV)
                wt("tensor", sV, m_sq)
                m_st = mm(pD[32:33, 0:512], onesc[:], sc1[:, 0:512], start=(j == 0),
                          stop=(j == NJ - 1), inc=sT)
            sc1_free_T = m_st
            ovl2_free_T = m_st
        stats_done = m_st

        wt("vector", sT, stats_done)
        ma = op("vector", lambda e: e.tensor_scalar(out=sc1[0:1, :], in0=pC[0:1, :],
           scalar1=1.0 / D, scalar2=None, op0=ALU.mult), inc=sV)
        mb = op("vector", lambda e: e.tensor_scalar(out=sc2[0:1, :], in0=pD[32:33, :],
           scalar1=1.0 / D, scalar2=None, op0=ALU.mult), inc=sV)
        wt("scalar", sV, ma)
        mc = op("scalar", lambda e: e.activation(out=sc3[0:1, :], in_=sc1[0:1, :],
           func=AF.Square), inc=sA)
        wt("vector", sA, mc)
        m_var1 = op("vector", lambda e: e.tensor_tensor(out=sc1[32:33, :], in0=sc2[0:1, :],
           in1=sc3[0:1, :], op=ALU.subtract), inc=sV)
        wt("scalar", sV, m_var1)
        m_l1 = op("scalar", lambda e: e.activation(out=sc3[32:33, :], in_=sc1[32:33, :],
           func=AF.Ln, bias=epsc[0:1, :]), inc=sA)
        wt("scalar", sA, m_l1)
        m_rstd1 = op("scalar", lambda e: e.activation(out=sc2[0:1, :], in_=sc3[32:33, :],
           func=AF.Exp, scale=-0.5), inc=sA)
        wt("tensor", sA, m_rstd1)
        m_bc = None
        for half in range(2):
            mm(pA[:, half * 512:(half + 1) * 512], ones1f[:], sc1[0:1, half * 512:(half + 1) * 512],
               start=True, stop=True)
            m_bc = mm(pB[:, half * 512:(half + 1) * 512], ones1f[:],
                      sc2[0:1, half * 512:(half + 1) * 512], start=True, stop=True,
                      inc=(sT if half == 1 else None))
        wt("vector", sT, m_bc)
        op("vector", lambda e: e.tensor_copy(ovl[:, 0:1024], pA[:]))
        bc_ready = op("vector", lambda e: e.tensor_copy(ovl[:, 1024:2048], pB[:]), inc=sV)

        prev_x1 = 0
        wt("sync", sV, bc_ready)
        dmaS(sc3[:], x1f_scr[0])
        for j in range(NJ if F_XN1 else 0):
            if prev_x1:
                wt("sync", sV, prev_x1)
            c_x1 = dmaS(sc3[:], x1f_scr[j])
            wt("vector", dS, c_x1)
            if j == 0:
                wt("vector", sV, bc_ready)
            op("vector", lambda e: e.tensor_tensor(out=sc2[:], in0=sc3[:], in1=ovl[:, 0:1024],
               op=ALU.subtract))
            op("vector", lambda e: e.tensor_tensor(out=ovl2, in0=sc2[:], in1=ovl[:, 1024:2048],
               op=ALU.mult))
            m_cp = op("vector", lambda e, j=j: e.tensor_copy(xnf[:, j * TOK:(j + 1) * TOK],
               ovl2), inc=sV)
            wt("vector", sV, m_cp)
            prev_x1 = op("vector", lambda e, j=j: e.reduce_sum(ut[:, 16 + j:17 + j], ovl2,
               axis=AXX), inc=sV)
        xn1_ready = prev_x1 if (F_XN1 and NJ > 0) else bc_ready

        def do_dump():
            dmp = os.environ.get("MK_DUMP", "x1")
            if dmp == "x1":
                c_dbg = dmaG(y_out.rearrange("t p (a k) -> (t p a) k", k=TOK)[0:NI * 128, :],
                             x1f_scr.rearrange("i p k -> (i p) k"))
                wt("gpsimd", dG, c_dbg)
            elif dmp == "ctx":
                c_dbg = dmaG(y_out.rearrange("t p d -> p t d"),
                             ctxf[:].rearrange("p (t d) -> p t d", d=D))
                wt("gpsimd", dG, c_dbg)
            elif dmp == "xn":
                c_dbg = dmaG(y_out.rearrange("t p d -> (t p) d"), xn_scr)
                wt("gpsimd", dG, c_dbg)
            elif dmp == "xnf":
                c_dbg = dmaG(y_out.rearrange("t p d -> p t d"),
                             xnf[:].rearrange("p (t d) -> p t d", d=D))
                wt("gpsimd", dG, c_dbg)
            elif dmp == "kv":
                c_dbg = dmaG(y_out.rearrange("t p d -> p t d")[:, 0:1, :],
                             kvK[:].rearrange("p (t d) -> p t d", d=D)[:, 0:1, :])
                c_dbg = dmaG(y_out.rearrange("t p d -> p t d")[:, 1:2, :],
                             kvV[:].rearrange("p (t d) -> p t d", d=D)[:, 0:1, :])
                c_dbg = dmaG(y_out.rearrange("t p d -> (t p) d")[256:288, 0:64].rearrange("a b -> (a b)").rearrange("(a b) -> a b", b=64),
                             ut[0:32, 0:64])
                wt("gpsimd", dG, c_dbg)
            elif dmp == "ut":
                c_dbg = dmaG(y_out.rearrange("t p d -> (t p) d")[0:128, 0:64], ut[:])
                wt("gpsimd", dG, c_dbg)
            elif dmp == "mb":
                c_dbg = dmaG(y_out.rearrange("t p d -> (t p) d")[0:128, :], ovl[:])
                wt("gpsimd", dG, c_dbg)

        if upto <= 2:
            dmp = os.environ.get("MK_DUMP", "x1")
            wt("gpsimd", sV, xn1_ready)
            if F_DBG and dmp in ("xnf", "ctx", "ut", "mb"):
                do_dump()
                _flush(block, EMIT)
                return nc
            if F_DBG and dmp == "x1":
                c_dbg = dmaG(y_out.rearrange("t p (a k) -> (t p a) k", k=TOK)[0:NI * 128, :],
                             x1f_scr.rearrange("i p k -> (i p) k"))
                wt("gpsimd", dG, c_dbg)
            elif F_DBG and dmp == "xn":
                c_dbg = dmaG(y_out.rearrange("t p d -> (t p) d"), xn_scr)
                wt("gpsimd", dG, c_dbg)
            elif F_DBG and dmp == "x0f":
                c_dbg = dmaG(y_out.rearrange("t p (a k) -> (t p a) k", k=TOK)[0:NI * 128, :],
                             x0f_scr.rearrange("i p k -> (i p) k"))
                wt("gpsimd", dG, c_dbg)
            elif F_DBG and dmp == "ctx":
                c_dbg = dmaG(y_out.rearrange("t p d -> p t d"),
                             ctxf[:].rearrange("p (t d) -> p t d", d=D))
                wt("gpsimd", dG, c_dbg)
            elif F_DBG and dmp == "ut":
                c_dbg = dmaS(y_out.rearrange("t p d -> (t p) d")[0:128, 0:64], ut[:])
                wt("sync", dS, c_dbg)
            _flush(block, EMIT)
            return nc

        P3T = int(os.environ.get("MK_P3T", "1"))
        for _eng in ("sync", "gpsimd", "tensor", "vector", "scalar"):
            wt(_eng, sV, xn1_ready)
        if P3T:
            c_tk = dmaS(ovl[:, 0:2048], t1k_in)
            dmaS(x1f_scr_t.ap()[0], ovl[:, 3072:4096])
        else:
            c_tk = dS.n
        if P3L <= 0:
            wt("sync", dS, dS.n)
            _flush(block, EMIT)
            return nc
        kspill = 0
        for j in range(8):
            wb = wS[j % 3]
            wt("gpsimd", sT, ws_free[j % 3])
            if int(os.environ.get("MK_P3W", "1")):
                c_w = dmaG(wb[:, 0:4096], wk1_in[j])
            else:
                c_w = dG.n
            wt("tensor", dG, c_w)
            wt("tensor", sV, xn1_ready if j == 0 else m_kr)
            m_k = None
            rhsbuf = xnf if P3K >= 1 else ctxf
            for half in range(2):
                for ic in range(NI):
                    m_k = mm(pA[:, half * 512:(half + 1) * 512], wb[:, ic * 128:(ic + 1) * 128],
                             rhsbuf[:, ic * TOK + half * 512:ic * TOK + (half + 1) * 512],
                             start=(ic == 0), stop=(ic == NI - 1),
                             inc=(sT if (ic == NI - 1 and half == 1) else None))
            ws_free[j % 3] = m_k
            wt("vector", sT, m_k)
            wt("vector", dS, c_tk)
            if kspill:
                wt("vector", dS, kspill)
            if P3K >= 2:
                op("vector", lambda e: e.tensor_tensor(out=sc3[:], in0=pA[:], in1=ovl[:, 0:1024], op=ALU.mult))
                op("vector", lambda e: e.tensor_tensor(out=sc1[0:64, :], in0=pA[64:128, :],
                   in1=ovl[0:64, 1024:2048], op=ALU.mult))
                op("vector", lambda e: e.tensor_tensor(out=sc1[64:128, :], in0=pA[0:64, :],
                   in1=ovl[64:128, 1024:2048], op=ALU.mult))
                m_kr = op("vector", lambda e: e.tensor_tensor(out=ptA[:], in0=sc3[:], in1=sc1[:],
                   op=ALU.add), inc=sV)
            else:
                m_kr = op("vector", lambda e: e.tensor_copy(ptA[:], pA[:]), inc=sV)
            wt("sync", sV, m_kr)
            if int(os.environ.get("MK_P3S", "1")) == 0:
                kspill = 0
            elif P3K >= 3:
                kspill = dmaS(cin_kv[j * 128:(j + 1) * 128, :], ptA[:])
            else:
                kspill = dmaS(cin_kv[0:128, :], ptA[:])
        if P3L <= 1:
            wt("gpsimd", dS, dS.n)
            wt("gpsimd", sV, sV.n)
            do_dump()
            _flush(block, EMIT)
            return nc
        vspill = kspill
        tpf = [0, 0]
        for j in range(8):
            wb = wS[j % 3]
            wt("gpsimd", sT, ws_free[j % 3])
            c_w = dmaG(wb[:, 0:4096], wv1_in[j])
            wt("tensor", dG, c_w)
            if j > 0:
                wt("tensor", sV, m_vc)
            m_v = None
            for half in range(2):
                for ic in range(NI):
                    m_v = mm(pB[:, half * 512:(half + 1) * 512], wb[:, ic * 128:(ic + 1) * 128],
                             xnf[:, ic * TOK + half * 512:ic * TOK + (half + 1) * 512],
                             start=(ic == 0), stop=(ic == NI - 1),
                             inc=(sT if (ic == NI - 1 and half == 1) else None))
            ws_free[j % 3] = m_v
            wt("vector", sT, m_v)
            if ovl2_free_T:
                wt("vector", sT, ovl2_free_T)
            m_vc = op("vector", lambda e: e.tensor_copy(ovl2, pB[:]), inc=sV)
            wt("tensor", sV, m_vc)
            if vspill:
                wt("tensor", dS, vspill)
            m_tp = None
            for g in range(2):
                pp = pC if g == 0 else pD
                if tpf[g]:
                    wt("tensor", sV, tpf[g])
                for u in range(4):
                    t = g * 4 + u
                    m_tp = op("tensor", lambda e, t=t, pp=pp, u=u: e.transpose(
                        pp[:, u * 128:(u + 1) * 128], ovl2[:, t * 128:(t + 1) * 128], idf[:]),
                        inc=(sT if u == 3 else None))
                wt("vector", sT, m_tp)
                tpf[g] = op("vector", lambda e, g=g, pp=pp: e.tensor_copy(
                    ptA[:, g * 512:(g + 1) * 512], pp[:, 0:512]), inc=sV)
            ovl2_free_T = m_tp
            wt("sync", sV, tpf[1])
            vspill = dmaS(
                cin_kv[1024:2048, j * 128:(j + 1) * 128].rearrange("(t p) c -> p t c", p=128),
                ptA[:].rearrange("p (t c) -> p t c", t=NT))
        if P3L <= 2:
            wt("gpsimd", dS, dS.n)
            wt("gpsimd", sV, sV.n)
            do_dump()
            _flush(block, EMIT)
            return nc
        wt("sync", sV, xn1_ready)
        c_m = dmaS(cin_m[:, 0:32], ut[:, 16:48])
        wt("gpsimd", dS, dS.n)
        if F_CC:
            op("gpsimd", lambda e: e.collective_compute(
                "AllGather", ALU.bypass, groups, [cin_kv_t.ap().opt()], [cout_kv_t.ap().opt()]), inc=cc)
            op("gpsimd", lambda e: e.collective_compute(
                "AllReduce", ALU.add, groups, [cin_m_t.ap().opt()], [cout_m_t.ap().opt()]), inc=cc)
        else:
            op("gpsimd", lambda e: e.dma_start(out=cout_m_t.ap(), in_=cin_m), inc=cc, inc_n=16)
            op("gpsimd", lambda e: e.dma_start(out=cout_kv[0:2048, :], in_=cin_kv), inc=cc, inc_n=16)
        cc_done = cc.n

        if P3L <= 3:
            wt("gpsimd", cc, cc_done)
            wt("gpsimd", sV, sV.n)
            do_dump()
            _flush(block, EMIT)
            return nc
        wt("sync", cc, cc_done)
        c_mo = dmaS(sc1[:, 0:32], cout_m)
        wt("vector", dS, c_mo)
        m_mean = op("vector", lambda e: e.tensor_scalar(out=ptB[:, 0:32], in0=sc1[:, 0:32],
           scalar1=1.0 / 2048.0, scalar2=None, op0=ALU.mult), inc=sV)
        wt("gpsimd", sT, kvK_free)
        wt("gpsimd", sT, ws_free[0])
        c_wqm = dmaG(kvK[:, 0:4096], wqm_in)
        wt("tensor", dG, c_wqm)
        wt("tensor", sV, m_mean)
        m_qm = None
        for ic in range(NI):
            m_qm = mm(pD[:, 0:1], kvK[:, ic * 128:(ic + 1) * 128], ptB[:, ic:ic + 1],
                      start=(ic == 0), stop=(ic == NI - 1), inc=(sT if ic == NI - 1 else None))
        wt("vector", sT, m_qm)
        m_qv = op("vector", lambda e: e.tensor_copy(ptB[:, 40:41], pD[:, 0:1]), inc=sV)
        wt("tensor", sV, m_qv)
        wt("tensor", dG, c_smallG)
        m_sc = mm(pA[0:1, 0:8], ptB[:, 40:41], keyss[:], start=True, stop=True, inc=sT)
        wt("vector", sT, m_sc)
        m_cpc = op("vector", lambda e: e.tensor_copy(sc2[0:1, 0:8], pA[0:1, 0:8]), inc=sV)
        wt("vector", sV, m_cpc)
        m_rmx = op("vector", lambda e: e.reduce_max(sc2[0:1, 8:9], sc2[0:1, 0:8], axis=AXX,
           negate=True), inc=sV)
        wt("vector", sV, m_rmx)
        m_nm = op("vector", lambda e: e.tensor_scalar(out=sc2[0:1, 8:9], in0=sc2[0:1, 8:9],
           scalar1=ISQ, scalar2=None, op0=ALU.mult), inc=sV)
        wt("scalar", sV, m_nm)
        m_ex = op("scalar", lambda e: e.activation(out=sc2[0:1, 0:8], in_=sc2[0:1, 0:8],
           func=AF.Exp, scale=ISQ, bias=sc2[0:1, 8:9]), inc=sA)
        wt("vector", sA, m_ex)
        m_cs = op("vector", lambda e: e.reduce_sum(sc2[0:1, 9:10], sc2[0:1, 0:8], axis=AXX), inc=sV)
        wt("vector", sV, m_cs)
        m_cr = op("vector", lambda e: e.reciprocal(sc2[0:1, 10:11], sc2[0:1, 9:10]), inc=sV)
        wt("vector", sV, m_cr)
        m_cf = op("vector", lambda e: e.tensor_scalar(out=sc2[0:1, 16:24], in0=sc2[0:1, 0:8],
           scalar1=sc2[0:1, 10:11], scalar2=None, op0=ALU.mult), inc=sV)
        wt("tensor", sV, m_cf)
        m_ct = mm(pB[0:8, 0:1], sc2[0:1, 16:24], idf[0:1, 0:1], start=True, stop=True, inc=sT)
        wt("vector", sT, m_ct)
        m_cc2 = op("vector", lambda e: e.tensor_copy(ptB[0:8, 512:513], pB[0:8, 0:1]), inc=sV)
        wt("tensor", sV, m_cc2)
        wt("tensor", dG, c_smallG)
        m_sel = None
        for q in range(4):
            m_sel = mm(pD[:, 8 + q:9 + q], sels[:, q * 128:(q + 1) * 128], ptB[0:8, 512:513],
                       start=True, stop=True, inc=(sT if q == 3 else None))
        wt("vector", sT, m_sel)
        m_ccol = op("vector", lambda e: e.tensor_copy(ut[:, 48:52], pD[:, 8:12]), inc=sV)

        if P3L <= 4:
            wt("gpsimd", sV, m_ccol)
            do_dump()
            _flush(block, EMIT)
            return nc
        hp_ready = 0
        for q in range(4):
            wb = wS[q % 3]
            wt("gpsimd", sT, ws_free[q % 3])
            c_w = dmaG(wb[:, 0:4096], ld1_in[q])
            wt("tensor", dG, c_w)
            wt("tensor", sV, m_ccol if q == 0 else hp_ready)
            m_h = None
            for half in range(2):
                for ic in range(NI):
                    m_h = mm(pC[:, half * 512:(half + 1) * 512], wb[:, ic * 128:(ic + 1) * 128],
                             xnf[:, ic * TOK + half * 512:ic * TOK + (half + 1) * 512],
                             start=(ic == 0), stop=(ic == NI - 1),
                             inc=(sT if (ic == NI - 1 and half == 1) else None))
            ws_free[q % 3] = m_h
            wt("vector", sT, m_h)
            hp_ready = op("vector", lambda e, q=q: e.tensor_scalar(out=hps[q],
               in0=pC[:], scalar1=ut[:, 48 + q:49 + q], scalar2=None, op0=ALU.mult), inc=sV)

        if upto <= 3:
            wt("gpsimd", sV, hp_ready)
            do_dump()
            _flush(block, EMIT)
            return nc
        c_t1q = dmaS(ovl[:, 0:2048], t1q_in)
        c_mk = dmaS(ovl[:, 2048:4096], mk_in.rearrange("c p d -> p c d"))
        kvK_free = m_qm
        kvV_free = 0
        sums_evac = hp_ready
        wt("sync", cc, cc_done)
        wt("sync", sT, m_qm)
        dmaS(ptA[:, 0:512], cout_kv[0:128, 0:512])
        for h in range(8):
            wt("sync", cc, cc_done)
            wt("sync", sT, kvK_free)
            dmaS(kvK[:, 0:2048].rearrange("p (s two c) -> p s two c", two=2, c=128)[:, :, 0, :],
                 cout_kv[h * 128:(h + 1) * 128, :].rearrange("p (s c) -> p s c", c=128))
            c_kt = dmaS(kvK[:, 0:2048].rearrange("p (s two c) -> p s two c", two=2, c=128)[:, :, 1, :],
                        cout_kv[2048 + h * 128:2048 + (h + 1) * 128, :].rearrange("p (s c) -> p s c", c=128))
            wt("sync", sT, kvV_free)
            dmaS(kvV[:, 0:2048].rearrange("p (s two c) -> p s two c", two=2, c=128)[:, :, 0, :],
                 cout_kv[1024:2048, h * 128:(h + 1) * 128].rearrange("(s p) c -> p s c", p=128))
            c_vt = dmaS(kvV[:, 0:2048].rearrange("p (s two c) -> p s two c", two=2, c=128)[:, :, 1, :],
                        cout_kv[3072:4096, h * 128:(h + 1) * 128].rearrange("(s p) c -> p s c", p=128))
            for qi in range(4):
                hq = 4 * h + qi
                wb = wS[hq % 3]
                wt("gpsimd", sT, ws_free[hq % 3])
                c_w = dmaG(wb[:, 0:4096], wq1_in[hq])
                wt("tensor", dG, c_w)
                if sums_evac:
                    wt("tensor", sV, sums_evac)
                m_q = None
                for half in range(2):
                    for ic in range(NI):
                        m_q = mm(pA[:, half * 512:(half + 1) * 512], wb[:, ic * 128:(ic + 1) * 128],
                                 xnf[:, ic * TOK + half * 512:ic * TOK + (half + 1) * 512],
                                 start=(ic == 0), stop=(ic == NI - 1),
                                 inc=(sT if (ic == NI - 1 and half == 1) else None))
                ws_free[hq % 3] = m_q
                wt("vector", sT, m_q)
                wt("vector", dS, c_t1q)
                op("vector", lambda e: e.tensor_tensor(out=sc3[:], in0=pA[:], in1=ovl[:, 0:1024], op=ALU.mult))
                op("vector", lambda e: e.tensor_tensor(out=sc1[0:64, :], in0=pA[64:128, :],
                   in1=ovl[0:64, 1024:2048], op=ALU.mult))
                op("vector", lambda e: e.tensor_tensor(out=sc1[64:128, :], in0=pA[0:64, :],
                   in1=ovl[64:128, 1024:2048], op=ALU.mult))
                q_rope = op("vector", lambda e, hq=hq: e.tensor_tensor(
                    out=ctxf[:, hq * TOK:(hq + 1) * TOK], in0=sc3[:], in1=sc1[:], op=ALU.add), inc=sV)

                wt("tensor", sV, q_rope)
                wt("tensor", dS, c_vt)
                wt("vector", dS, c_mk)
                st_marks = {}
                msk_marks = {}
                exp_marks = {}
                pv_marks = {}

                def col_pieces(a):
                    if a < 512:
                        return [(a, 512), (512, 1024)]
                    return [(a, 1024)]

                def emit_st(c, hq=hq, st_marks=st_marks, exp_marks=exp_marks):
                    ps = pB if c % 2 == 0 else pC
                    a = (c // 2) * 128
                    if c >= 2:
                        wt("tensor", sA, exp_marks[c - 2])
                    for (lo, hi) in col_pieces(a):
                        st_marks[c] = mm(ps[:, lo:hi], kvK[:, c * 128:(c + 1) * 128],
                                         ctxf[:, hq * TOK + lo:hq * TOK + hi],
                                         start=True, stop=True, inc=sT)

                def emit_mask(c, st_marks=st_marks, msk_marks=msk_marks):
                    ps = pB if c % 2 == 0 else pC
                    a = (c // 2) * 128
                    wt("vector", sT, st_marks[c])
                    msk_marks[c] = op("vector", lambda e, ps=ps, a=a, c=c: e.tensor_tensor(
                        out=ps[:, a:a + 128], in0=ps[:, a:a + 128],
                        in1=ovl[:, 2048 + c * 128:2048 + (c + 1) * 128], op=ALU.add), inc=sV)

                def emit_exp(c, msk_marks=msk_marks, exp_marks=exp_marks, pv_marks=pv_marks):
                    ps = pB if c % 2 == 0 else pC
                    pt = ptA if c % 2 == 0 else ptB
                    a = (c // 2) * 128
                    wt("scalar", sV, msk_marks[c])
                    if c >= 2:
                        wt("scalar", sT, pv_marks[c - 2])
                    exp_marks[c] = op("scalar", lambda e, ps=ps, pt=pt, a=a: e.activation(
                        out=pt[:, a:1024], in_=ps[:, a:1024], func=AF.Exp), inc=sA)

                def emit_pv(c, exp_marks=exp_marks, pv_marks=pv_marks):
                    pt = ptA if c % 2 == 0 else ptB
                    a = (c // 2) * 128
                    wt("tensor", sA, exp_marks[c])
                    for (lo, hi) in col_pieces(a):
                        mm(pD[:, lo:hi], kvV[:, c * 128:(c + 1) * 128], pt[:, lo:hi],
                           start=(c == 0), stop=(c == 15))
                        pv_marks[c] = mm(pA[:, lo:hi], ones_bf[:], pt[:, lo:hi],
                                         start=(c == 0), stop=(c == 15), inc=sT)

                emit_st(0)
                emit_mask(0)
                emit_exp(0)
                for c in range(1, 16):
                    emit_st(c)
                    emit_mask(c)
                    emit_exp(c)
                    emit_pv(c - 1)
                emit_pv(15)
                kvK_free = st_marks[15]
                kvV_free = pv_marks[15]
                wt("vector", sT, pv_marks[15])
                wt("vector", sV, sV.n)
                m_rc = op("vector", lambda e: e.reciprocal(sc3[:], pA[:]), inc=sV)
                wt("vector", sV, m_rc)
                sums_evac = op("vector", lambda e, hq=hq: e.tensor_tensor(
                    out=ctxf[:, hq * TOK:(hq + 1) * TOK], in0=pD[:], in1=sc3[:],
                    op=ALU.mult), inc=sV)

        p4_v = sV.n
        if upto <= 4:
            wt("gpsimd", sV, p4_v)
            do_dump()
            _flush(block, EMIT)
            return nc

        pA_free, pA_free_ctr = p4_v, sV
        pB_free, pB_free_ctr = p4_v, sV
        out_slot_free = [0, 0]
        ovl5_busy_T = 0
        prev_x2 = 0
        final_out = 0
        wt("sync", sV, p4_v)
        dmaS(x1f_scr_t.ap()[0], ovl[:, 3072:4096])
        dmaS(sc3[:], x1f_scr[0])
        for j in range(NI):
            wg_buf = wS[j % 2]
            wt("gpsimd", sT, ws_free[j % 2])
            c_wg = dmaG(wg_buf[:, 0:4608], wg1_in[j])
            wt("gpsimd", sT, ws_free[2])
            c_wo = dmaG(wS[2][:, 0:4096], wo1_in[j])
            wt("tensor", dG, c_wg)
            wt("tensor", pA_free_ctr, pA_free)
            m_gm = None
            for half in range(2):
                for ic in range(36):
                    rhs = (xnf[:, ic * TOK + half * 512:ic * TOK + (half + 1) * 512]
                           if ic < 32
                           else hps[ic - 32][:, half * 512:(half + 1) * 512])
                    m_gm = mm(pA[:, half * 512:(half + 1) * 512],
                              wg_buf[:, ic * 128:(ic + 1) * 128], rhs,
                              start=(ic == 0), stop=(ic == 35),
                              inc=(sT if (ic == 35 and half == 1) else None))
            ws_free[j % 2] = m_gm
            wt("tensor", dG, c_wo)
            wt("tensor", pB_free_ctr, pB_free)
            m_sm = None
            for half in range(2):
                for ic in range(NI):
                    m_sm = mm(pB[:, half * 512:(half + 1) * 512],
                              wS[2][:, ic * 128:(ic + 1) * 128],
                              ctxf[:, ic * TOK + half * 512:ic * TOK + (half + 1) * 512],
                              start=(ic == 0), stop=(ic == NI - 1),
                              inc=(sT if (ic == NI - 1 and half == 1) else None))
            ws_free[2] = m_sm
            wt("scalar", sT, m_gm)
            if prev_x2:
                wt("scalar", sV, prev_x2)
            m_silu = op("scalar", lambda e: e.activation(out=sc1[:], in_=pA[:], func=AF.Silu), inc=sA)
            pA_free, pA_free_ctr = m_silu, sA
            if ovl5_busy_T:
                wt("sync", sT, ovl5_busy_T)
            if prev_x2:
                wt("sync", sV, prev_x2)
            c_x1p = dmaS(sc3[:], x1f_scr[j])
            wt("vector", sA, m_silu)
            wt("vector", sT, m_sm)
            wt("vector", dS, c_x1p)
            if ovl5_busy_T:
                wt("vector", sT, ovl5_busy_T)
            op("vector", lambda e: e.tensor_tensor(out=sc2[:], in0=pB[:], in1=sc1[:], op=ALU.mult))
            m_x2 = op("vector", lambda e: e.tensor_tensor(out=ovl5, in0=sc2[:], in1=sc3[:],
               op=ALU.add), inc=sV)
            pB_free, pB_free_ctr = m_x2, sV
            prev_x2 = m_x2
            wt("tensor", sV, m_x2)
            for g in range(2):
                pp = pC if g == 0 else pD
                if out_slot_free[g]:
                    wt("tensor", sV, out_slot_free[g])
                m_tp = None
                for u in range(4):
                    t = g * 4 + u
                    m_tp = op("tensor", lambda e, t=t, pp=pp, u=u: e.transpose(
                        pp[:, u * 128:(u + 1) * 128], ovl5[:, t * 128:(t + 1) * 128], idf[:]),
                        inc=(sT if u == 3 else None))
                if g == 1:
                    ovl5_busy_T = m_tp
                wt("vector", sT, m_tp)
                out_slot_free[g] = op("vector", lambda e, g=g, pp=pp: e.tensor_copy(
                    sc2[:, g * 512:(g + 1) * 512], pp[:, 0:512]), inc=sV)
            wt("sync", sV, out_slot_free[1])
            final_out = dmaS(
                y_out.rearrange("t p (a c) -> p t a c", c=128)[:, :, j, :],
                sc2[:].rearrange("p (t c) -> p t c", t=NT))
        wt("sync", dS, final_out)
        if os.environ.get("MK_CANARY"):
            c_c1 = dmaS(xn_scr[0:128, 0:368].bitcast(F32), canhi[:])
            c_c2 = dmaS(xn_scr[128:256, 0:1140].bitcast(F32), guard[:])
            cbuf = sc1
            wt("vector", dS, max(c_c1, c_c2))
            op("vector", lambda e: e.tensor_copy(cbuf[:, 0:184], canhi[:]))
            m_cn = op("vector", lambda e: e.tensor_copy(cbuf[:, 184:754], guard[:]), inc=sV)
            wt("sync", sV, m_cn)
            c_c3 = dmaS(y_out[0][0:128, 0:754], cbuf[:, 0:754])
            wt("sync", dS, c_c3)

        _flush(block, EMIT)
    return nc


def _flush(block, EMIT):
    @block.sync
    def _(e):
        for f in EMIT["sync"]:
            f(e)

    @block.gpsimd
    def _(e):
        for f in EMIT["gpsimd"]:
            f(e)

    @block.tensor
    def _(e):
        for f in EMIT["tensor"]:
            f(e)

    @block.vector
    def _(e):
        for f in EMIT["vector"]:
            f(e)

    @block.scalar
    def _(e):
        for f in EMIT["scalar"]:
            f(e)



def _tile_w(w):
    I, O = w.shape
    ni, nj = I // 128, O // 128
    t = w.reshape(ni, 128, nj, 128).transpose(2, 1, 0, 3).reshape(nj, 128, I)
    return np.ascontiguousarray(t)


def _rope_tables(pos, kind):
    HD = 128
    if kind == "l0":
        inv = 1.0 / (10000.0 ** (np.arange(0, HD, 2, dtype=np.float32) / HD))
        inv = inv.astype(np.float32)
    else:
        TH, FACTOR, HI_FF, LO_FF, OM = 500000.0, 8.0, 4.0, 1.0, 8192
        inv = 1.0 / TH ** (np.arange(0, HD, 2, dtype=np.float64) / HD)
        wav = 2.0 * np.pi / inv
        low_wl = OM / LO_FF
        high_wl = OM / HI_FF
        smooth = (OM / wav - LO_FF) / (HI_FF - LO_FF)
        scaled = np.where(wav > low_wl, inv / FACTOR, inv)
        mid = (wav <= low_wl) & (wav >= high_wl)
        scaled = np.where(mid, (1 - smooth) * inv / FACTOR + smooth * inv, scaled)
        inv = scaled.astype(np.float32)
    freqs = np.outer(pos.astype(np.float32), inv)
    emb = np.concatenate([freqs, freqs], axis=-1)
    cos = np.cos(emb).astype(np.float32)
    sin = np.sin(emb).astype(np.float32)
    sgn = np.ones((1, 128), np.float32)
    sgn[0, :64] = -1.0
    return np.ascontiguousarray(cos.T), np.ascontiguousarray((sin * sgn).T)


def _masks(parity):
    mk = np.zeros((16, 128, 128), np.float32)
    tri = np.where(np.arange(128)[:, None] <= np.arange(128)[None, :], 0.0, NEG).astype(np.float32)
    deny = np.full((128, 128), NEG, np.float32)
    for c in range(16):
        smin = c // 2
        g = 2 * smin + parity
        if g == c:
            mk[c] = tri
        elif g < c:
            mk[c] = deny
    return mk


def kernel(L0_attn_hidden_states, L0_attn_k_cache, L0_attn_v_cache, L0_attn_Wq, L0_attn_Wo,
           L0_attn_block_table, L0_ffn_W, L0_ffn_b, L0_ffn_A, L0_ffn_Bm, L0_ffn_Wr,
           L1_attn_wq, L1_attn_wk, L1_attn_wv, L1_attn_wo,
           L1_ffn_W_org, L1_ffn_lora_down, L1_ffn_lora_up, L1_ffn_lora_keys, L1_ffn_Wq):
    f = np.ascontiguousarray
    x = np.asarray(L0_attn_hidden_states, np.float32)
    kc = f(np.asarray(L0_attn_k_cache, np.float32).reshape(512, 16384))
    vc = f(np.asarray(L0_attn_v_cache, np.float32).reshape(512, 16384))
    bt = np.asarray(L0_attn_block_table, np.int32)
    btc = f(bt.reshape(2, 128).T).astype(np.int32)

    wq0 = _tile_w(f(np.asarray(L0_attn_Wq, np.float32).T))
    wo0 = _tile_w(f(np.asarray(L0_attn_Wo, np.float32).T))
    bmt = np.asarray(L0_ffn_Bm, np.float32).transpose(0, 2, 1).reshape(32, 4096)
    bmt_pad = np.concatenate([bmt, np.zeros((96, 4096), np.float32)], axis=0)
    wg0 = _tile_w(f(np.concatenate([np.asarray(L0_ffn_W, np.float32).T, bmt_pad], axis=0)))
    bfm_h = f(np.asarray(L0_ffn_b, np.float32).reshape(32, 128).T)
    wrt = f(np.asarray(L0_ffn_Wr, np.float32).T.reshape(32, 128, 4).transpose(1, 0, 2).reshape(128, 128))
    att = f(np.asarray(L0_ffn_A, np.float32).reshape(32, 4096).T
            .reshape(32, 128, 32).transpose(1, 0, 2).reshape(128, 1024))
    e8 = np.zeros((4, 32), np.float32)
    for e in range(4):
        e8[e, 8 * e:8 * e + 8] = 2.0

    wq1 = _tile_w(f(np.asarray(L1_attn_wq, np.float32)))
    wk1 = _tile_w(f(np.asarray(L1_attn_wk, np.float32)))
    wv1 = _tile_w(f(np.asarray(L1_attn_wv, np.float32)))
    wo1 = _tile_w(f(np.asarray(L1_attn_wo, np.float32)))
    lut = np.asarray(L1_ffn_lora_up, np.float32).reshape(512, 4096)
    wg1 = _tile_w(f(np.concatenate([np.asarray(L1_ffn_W_org, np.float32).T, lut], axis=0)))
    ld1 = _tile_w(f(np.asarray(L1_ffn_lora_down, np.float32).transpose(1, 0, 2).reshape(4096, 512)))
    wqm = f(np.asarray(L1_ffn_Wq, np.float32).T.reshape(32, 128, 128)
            .transpose(1, 0, 2).reshape(128, 4096))
    keys = f(np.asarray(L1_ffn_lora_keys, np.float32).T)
    sel = np.zeros((8, 512), np.float32)
    for q in range(4):
        for p in range(128):
            sel[2 * q + p // 64, q * 128 + p] = 1.0
    idf = np.eye(128, dtype=np.float32)

    import os
    nc = build_nc(upto=int(os.environ.get("MK_UPTO", "6")))
    in_maps = []
    for c in range(8):
        b, par = c // 2, c % 2
        tiles = np.arange(par, 16, 2)
        x0c = f(x[b].reshape(16, 128, D)[tiles])
        pos = (tiles[:, None] * 128 + np.arange(128)[None, :]).reshape(-1)
        c0, s0 = _rope_tables(pos, "l0")
        c1, s1 = _rope_tables(pos, "l1")
        t0 = f(np.concatenate([c0 * ISQ, s0 * ISQ], axis=1))
        t1q = f(np.concatenate([c1 * ISQ, s1 * ISQ], axis=1))
        t1k = f(np.concatenate([c1, s1], axis=1))
        mk = _masks(par)
        in_maps.append(dict(
            x0_in=x0c, kc_in=kc, vc_in=vc, btc_in=btc,
            wq0_in=wq0, wo0_in=wo0, wg0_in=wg0, bfm_in=bfm_h, wrt_in=wrt, att_in=att,
            e8_in=e8, wq1_in=wq1, wk1_in=wk1, wv1_in=wv1, wo1_in=wo1, wg1_in=wg1,
            ld1_in=ld1, wqm_in=wqm, keys_in=keys, sel_in=sel,
            t0_in=t0, t1q_in=t1q, t1k_in=t1k, mk_in=mk, idf_in=idf))

    res = run_bass_kernel_spmd(nc, in_maps, core_ids=list(range(8)))

    out = np.empty((4, 2048, D), np.float32)
    for c in range(8):
        b, par = c // 2, c % 2
        tiles = np.arange(par, 16, 2)
        yc = np.asarray(res.results[c]["y_out"])
        ob = out[b].reshape(16, 128, D)
        ob[tiles] = yc
    return out
